# version 07
# baseline (speedup 1.0000x reference)
import contextlib

import numpy as np
import ml_dtypes

import concourse.bass as bass
import concourse.mybir as mybir
from concourse.bass_utils import run_bass_kernel_spmd

F32 = mybir.dt.float32
BF16 = mybir.dt.bfloat16
I32 = mybir.dt.int32
AF = mybir.ActivationFunctionType
ALU = mybir.AluOpType
AX = mybir.AxisListType


def full_cfg():
    return dict(
        S=4096, D=4096, HL=16, DN=128, DV=128, E=8, R=16, O=4096,
        W0=611, W1=4095,
        SCALE=192.0 ** -0.5, SHIFT=12.0, EPS=1e-5, LORA_SCALING=2.0,
        P1GW=128,
        P2GW=512,
        OQN=8,
        DEBUG_TAPS=(),
    )


def small_cfg():
    return dict(
        S=512, D=512, HL=2, DN=128, DV=128, E=8, R=16, O=512,
        W0=77, W1=511,
        SCALE=192.0 ** -0.5, SHIFT=12.0, EPS=1e-5, LORA_SCALING=2.0,
        P1GW=128, P2GW=512, OQN=2,
        DEBUG_TAPS=(),
    )


class C:

    def __init__(self, cfg):
        self.__dict__.update(cfg)
        self.cfg = cfg
        self.NDC = self.D // 128
        self.NTC = self.S // 128
        self.QS = self.S // 4
        self.HDV = 2 * self.HL * self.DV
        self.ER = self.E * self.R
        assert self.ER == 128
        self.WIN_N = self.W1 - self.W0
        self.MYTOK = 2 * self.QS
        self.NMT = self.MYTOK // 128
        self.NFT = self.QS // 128
        self.OQ = self.O // self.OQN
        self.NP1 = self.S // self.P1GW
        self.NP2 = self.S // self.P2GW
        self.HLDN = self.HL * self.DN
        self.NCC = self.HLDN // 128


def _finish(block, _):
    return


def build_program(cfg):
    c = C(cfg)
    S, D, HL, DN, DV = c.S, c.D, c.HL, c.DN, c.DV
    E, R, O, ER = c.E, c.R, c.O, c.ER
    NDC, NTC, QS = c.NDC, c.NTC, c.QS
    taps = set(cfg.get("DEBUG_TAPS", ()))

    nc = bass.Bass("TRN2", target_bir_lowering=False,
                   detect_race_conditions=cfg.get("RACE_DETECT", False))

    xT = nc.dram_tensor("xT", [D, S], F32, kind="ExternalInput").ap()
    x_strips = nc.dram_tensor("x_strips", [c.MYTOK, D], F32, kind="ExternalInput").ap()
    wq_l = nc.dram_tensor("wq_l", [D, c.HLDN], F32, kind="ExternalInput").ap()
    wk_l = nc.dram_tensor("wk_l", [D, c.HLDN], F32, kind="ExternalInput").ap()
    wv_l = nc.dram_tensor("wv_l", [D, HL * DV], F32, kind="ExternalInput").ap()
    wo_full = nc.dram_tensor("wo_full", [c.HDV, D], F32, kind="ExternalInput").ap()
    WTt = nc.dram_tensor("WTt", [D, O], F32, kind="ExternalInput").ap()
    bTt = nc.dram_tensor("bTt", [1, O], F32, kind="ExternalInput").ap()
    ATt = nc.dram_tensor("ATt", [D, ER], F32, kind="ExternalInput").ap()
    BTt = nc.dram_tensor("BTt", [ER, O], F32, kind="ExternalInput").ap()
    rwHt = nc.dram_tensor("rwHt", [128, E * NDC], F32, kind="ExternalInput").ap()
    rbTt = nc.dram_tensor("rbTt", [1, E], F32, kind="ExternalInput").ap()
    wmaskt = nc.dram_tensor("wmaskt", [128, c.MYTOK], F32, kind="ExternalInput").ap()
    prm = nc.dram_tensor("prm", [1, 12], I32, kind="ExternalInput").ap()

    out_t = nc.dram_tensor("out", [QS, O], F32, kind="ExternalOutput").ap()

    lnT_h = nc.dram_tensor("lnT_h", [D, S], BF16, kind="Internal").ap()
    qT_h = nc.dram_tensor("qT_h", [c.HLDN, S], BF16, kind="Internal").ap()
    kT_h = nc.dram_tensor("kT_h", [c.HLDN, S], BF16, kind="Internal").ap()
    v_h = nc.dram_tensor("v_h", [S, HL * DV], BF16, kind="Internal").ap()
    xaft_h = nc.dram_tensor("xaft_h", [c.MYTOK, D], F32, kind="Internal").ap()
    attnT_loc = nc.dram_tensor("attnT_loc", [c.HL * 128, S], BF16, kind="Internal").ap()
    attnT_l2 = nc.dram_tensor("attnT_l2", [c.HDV, c.MYTOK], BF16, kind="Internal").ap()
    ln2T_h = nc.dram_tensor("ln2T_h", [D, c.MYTOK], BF16, kind="Internal").ap()

    sh_attnT = nc.dram_tensor("sh_attnT", [c.HDV, S], BF16, kind="Internal",
                              addr_space="Shared").ap()
    sh_qpart = nc.dram_tensor("sh_qpart", [2 * 128, NDC], F32, kind="Internal",
                              addr_space="Shared").ap()
    sh_flags = nc.dram_tensor("sh_flags", [2, 16], I32, kind="Internal",
                              addr_space="Shared").ap()

    ident_c = nc.inline_tensor(np.eye(128, dtype=ml_dtypes.bfloat16), "ident_c").ap()
    ones_colb = nc.inline_tensor(np.ones((128, 1), dtype=ml_dtypes.bfloat16), "ones_colb").ap()
    ones_rowb = nc.inline_tensor(np.ones((1, 128), dtype=ml_dtypes.bfloat16), "ones_rowb").ap()
    ones_colf = nc.inline_tensor(np.ones((128, 1), dtype=np.float32), "ones_colf").ap()
    ones_rowf = nc.inline_tensor(np.ones((1, 128), dtype=np.float32), "ones_rowf").ap()
    cm = np.zeros((128, 4 * 512), dtype=ml_dtypes.bfloat16)
    for j in range(4):
        cm[:, j * 512:(j + 1) * 512] = (
            np.arange(128)[:, None] + j * 128 <= np.arange(512)[None, :])
    cmask_c = nc.inline_tensor(cm, "cmask_c").ap()
    rep = np.zeros((E, ER), dtype=ml_dtypes.bfloat16)
    for e in range(E):
        rep[e, e * R:(e + 1) * R] = np.float32(cfg["LORA_SCALING"])
    rep_c = nc.inline_tensor(rep, "rep_c").ap()

    CVALS = [0.0, 1.0, -1.0, float(cfg["SCALE"]), -float(cfg["SHIFT"]),
             float(cfg["EPS"]), -0.5, 1.0 / D, 1.0 / c.WIN_N, -1.0]
    cvals_c = nc.inline_tensor(
        np.tile(np.array([CVALS], dtype=np.float32), (128, 1)), "cvals_c").ap()

    st_ = contextlib.ExitStack()

    def sb(name, shape, dtype):
        return st_.enter_context(nc.sbuf_tensor(name, shape, dtype))

    def mkpsum(name, shape, dtype=F32):
        return st_.enter_context(nc.psum_tensor(name, shape, dtype))

    def mksem(name):
        return st_.enter_context(nc.semaphore(name))

    with st_:
        P1GW, P2GW = c.P1GW, c.P2GW
        big = sb("big", [128, 32768], BF16)
        ar2 = sb("ar2", [128, 16384], BF16)
        arf = sb("arf", [128, NDC * P1GW], F32)
        attsb = sb("attsb", [128, 2 * (c.HDV // 128) * 128], BF16)
        wkb = sb("wkb", [128, S], BF16)
        wkb2 = sb("wkb2", [128, S], BF16)
        vbuf = sb("vbuf", [128, S], BF16)
        pt2 = sb("pt2", [128, 2048], BF16)
        attc = sb("attc", [128, 1024], BF16)
        evb = sb("evb", [128, 2048], BF16)
        EVW = max(c.O // 4, c.OQ)
        evf = sb("evf", [128, 2 * EVW], F32)
        xres = sb("xres", [128, 2 * EVW], F32)
        wk1 = sb("wk1", [128, 512], F32)
        bcv = sb("bcv", [128, 1024], F32)
        st_m = sb("st_m", [1, 512], F32)
        st_sq = sb("st_sq", [1, 512], F32)
        st_var = sb("st_var", [1, 512], F32)
        st_rstd = sb("st_rstd", [1, 512], F32)
        cstv = sb("cstv", [128, len(CVALS)], F32)
        idn = sb("idn", [128, 128], BF16)
        onebc = sb("onebc", [128, 1], BF16)
        onebr = sb("onebr", [1, 128], BF16)
        onefr = sb("onefr", [1, 128], F32)
        onefc = sb("onefc", [128, 1], F32)
        cmv = sb("cmv", [128, 4 * 512], BF16)
        repv = sb("repv", [E, ER], BF16)
        rbv = sb("rbv", [1, E], F32)
        rwv = sb("rwv", [128, E * NDC], F32)
        prmv = sb("prmv", [1, 12], I32)
        flgrd = sb("flgrd", [1, 4], I32)
        wmv = wkb2
        atv = vbuf
        tts = sb("tts", [128, c.NFT * 128], BF16)
        ln2s = sb("ln2s", [128, 16], F32)
        qpart = sb("qpart", [128, NDC], F32)
        qpeer = sb("qpeer", [128, NDC], F32)
        r8 = sb("r8", [1, 64], F32)
        r8b = sb("r8b", [8, 8], BF16)
        rexp = sb("rexp", [128, 1], F32)

        for i, v in enumerate(CVALS):
            nc.const_aps.register(F32, float(v), cstv[:, i:i + 1])

        ps_sc0 = mkpsum("ps_sc0", [128, 1024])
        ps_sc1 = mkpsum("ps_sc1", [128, 1024])
        ps4t = mkpsum("ps4t", [128, 512])
        ps = [ps_sc0.ap()[:, 0:512], ps_sc0.ap()[:, 512:1024],
              ps_sc1.ap()[:, 0:512], ps_sc1.ap()[:, 512:1024],
              ps4t.ap()[:, 0:512]]
        ps_sumA = mkpsum("ps_sumA", [1, 512])
        ps_tr = mkpsum("ps_tr", [128, 1024], BF16)
        ps_bc = ps_tr.ap().bitcast(F32)
        ps_tr2 = mkpsum("ps_tr2", [128, 1024], BF16)
        ps_av1 = ps_tr2.ap().bitcast(F32)

        s_hw = mksem("s_hw")
        s_st = mksem("s_st")
        s_sw = mksem("s_sw")
        s_ac2 = mksem("s_ac2")
        s_pe = mksem("s_pe")
        s_ve = mksem("s_ve")
        s_ac = mksem("s_ac")
        s_gp = mksem("s_gp")
        s_po1 = mksem("s_po1")
        s_po2 = mksem("s_po2")

        cnt = {}
        for s in (s_hw, s_st, s_sw, s_ac2, s_pe, s_ve, s_ac, s_gp):
            cnt[id(s)] = 0

        def inc(instr, s, n=1):
            cnt[id(s)] += n
            return instr.then_inc(s, n)

        M = {}

        def mark(name, s):
            M[name] = (s, cnt[id(s)])

        def wait(eng, name, extra=0):
            s, v = M[name]
            if v + extra > 0:
                eng.wait_ge(s, v + extra)

        block = st_.enter_context(nc.Block())
        gpregs = {}

        def p0_sync(sync):
            for dst, src in ((idn, ident_c), (onebc, ones_colb), (onebr, ones_rowb),
                             (onefr, ones_rowf), (onefc, ones_colf),
                             (cmv, cmask_c), (repv, rep_c),
                             (rbv, rbTt), (rwv, rwHt), (cstv, cvals_c),
                             (prmv, prm)):
                inc(sync.dma_start(out=dst[:, :], in_=src[:, :]), s_hw, 16)
        block.sync(p0_sync)
        mark("p0_const", s_hw)


        GW = c.P1GW
        NP1 = c.NP1
        assert 4 * GW <= 512
        arf2 = attsb.bitcast(F32)
        BGR = 2 * NDC * GW

        def p1_arf(par):
            return arf if par == 0 else arf2

        ve_ord = [("cast", 0), ("cast", 1)]
        ac_ord = [("sq", 0), ("sq", 1)]
        pe_ord = [("stat", 0)]
        hw_ord = [("ld", 0), ("ld", 1)]
        st_ord = []
        for g in range(NP1):
            ve_ord += [("vstat", g), ("vstat2", g), ("apply", g)]
            if g + 2 < NP1:
                ve_ord.append(("cast", g + 2))
            ac_ord += [("astat", g), ("bce", g)]
            if g + 2 < NP1:
                ac_ord.append(("sq", g + 2))
            if g + 1 < NP1:
                pe_ord.append(("stat", g + 1))
            pe_ord.append(("bc", g))
            st_ord.append(("st", g))
            if g + 2 < NP1:
                hw_ord.append(("ld", g + 2))

        p1b = dict(hw=cnt[id(s_hw)], st=cnt[id(s_st)], ve=cnt[id(s_ve)],
                   ac=cnt[id(s_ac)], pe=cnt[id(s_pe)])
        VE_I = {k: p1b["ve"] + i + 1 for i, k in enumerate(ve_ord)}
        AC_I = {k: p1b["ac"] + i + 1 for i, k in enumerate(ac_ord)}
        PE_I = {k: p1b["pe"] + i + 1 for i, k in enumerate(pe_ord)}
        HW_I = {k: p1b["hw"] + (i + 1) * 16 for i, k in enumerate(hw_ord)}
        ST_I = {k: p1b["st"] + (i + 1) * 16 for i, k in enumerate(st_ord)}

        def p1_sync(sync):
            wait(sync, "p0_const")
            for kind, g in hw_ord + []:
                pass
            def ld(g):
                par = g % 2
                if g >= 2:
                    sync.wait_ge(s_ve, VE_I[("apply", g - 2)])
                inc(sync.dma_start(
                    out=p1_arf(par)[:, 0:NDC * GW]
                        .rearrange("p (dc t) -> p dc t", t=GW),
                    in_=xT[0:D, g * GW:(g + 1) * GW]
                        .rearrange("(dc p) t -> p dc t", p=128)), s_hw, 16)

            def stg(g):
                par = g % 2
                sync.wait_ge(s_ve, VE_I[("apply", g)])
                inc(sync.dma_start(
                    out=lnT_h[0:D, g * GW:(g + 1) * GW]
                        .rearrange("(dc p) t -> p dc t", p=128),
                    in_=ar2[:, par * NDC * GW:(par + 1) * NDC * GW]
                        .rearrange("p (dc t) -> p dc t", t=GW)), s_st, 16)

            ld(0)
            ld(1)
            for g in range(NP1):
                if g + 2 < NP1:
                    ld(g + 2)
                stg(g)
        block.sync(p1_sync)

        def p1_ve(v):
            def cast(g):
                par = g % 2
                v.wait_ge(s_hw, HW_I[("ld", g)])
                if g >= 2:
                    v.wait_ge(s_pe, PE_I[("stat", g - 2)])
                v.tensor_copy(big[:, par * BGR:par * BGR + NDC * GW],
                              p1_arf(par)[:, 0:NDC * GW])
                inc(v.drain(), s_ve)

            def vstat(g):
                par = g % 2
                v.wait_ge(s_pe, PE_I[("stat", g)])
                pss = ps_sumA[0:1, 0:512] if par == 0 else ps[4][0:1, 0:512]
                o = par * GW
                v.tensor_scalar_mul(st_m[0:1, o:o + GW], pss[0:1, 0:GW], 1.0 / D)
                v.tensor_scalar_mul(st_sq[0:1, o:o + GW], pss[0:1, GW:2 * GW], 1.0 / D)
                v.drain()
                v.tensor_tensor(st_var[0:1, o:o + GW], st_m[0:1, o:o + GW],
                                st_m[0:1, o:o + GW], ALU.mult)
                v.drain()
                v.tensor_tensor(st_var[0:1, o:o + GW], st_sq[0:1, o:o + GW],
                                st_var[0:1, o:o + GW], ALU.subtract)
                inc(v.drain(), s_ve)

            def vstat2(g):
                par = g % 2
                o = par * GW
                v.wait_ge(s_ac, AC_I[("astat", g)])
                v.tensor_tensor(st_m[0:1, o:o + GW], st_m[0:1, o:o + GW],
                                st_rstd[0:1, o:o + GW], ALU.mult)
                v.drain()
                v.tensor_scalar_mul(st_m[0:1, o:o + GW], st_m[0:1, o:o + GW], -1.0)
                inc(v.drain(), s_ve)

            def apply(g):
                par = g % 2
                o = par * 2 * GW
                v.wait_ge(s_ac, AC_I[("bce", g)])
                if g >= 2:
                    v.wait_ge(s_st, ST_I[("st", g - 2)])
                a3 = p1_arf(par)[:, 0:NDC * GW].rearrange("p (o t) -> p o t", t=GW)
                o3 = ar2[:, par * NDC * GW:(par + 1) * NDC * GW] \
                    .rearrange("p (o t) -> p o t", t=GW)
                rb = wk1[:, o:o + GW].rearrange("p (q t) -> p q t", q=1) \
                    .broadcast_to([128, NDC, GW])
                mb = wk1[:, o + GW:o + 2 * GW].rearrange("p (q t) -> p q t", q=1) \
                    .broadcast_to([128, NDC, GW])
                v.tensor_tensor(a3, a3, rb, ALU.mult)
                v.drain()
                v.tensor_tensor(o3, a3, mb, ALU.add)
                inc(v.drain(), s_ve)

            fns = dict(cast=cast, vstat=vstat, vstat2=vstat2, apply=apply)
            for kind, g in ve_ord:
                fns[kind](g)
        block.vector(p1_ve)

        def p1_ac(sc):
            def sq(g):
                par = g % 2
                sc.wait_ge(s_ve, VE_I[("cast", g)])
                if g >= 2:
                    sc.wait_ge(s_pe, PE_I[("stat", g - 2)])
                sc.activation(big[:, par * BGR + NDC * GW:par * BGR + 2 * NDC * GW],
                              big[:, par * BGR:par * BGR + NDC * GW],
                              AF.Square, bias=0.0, scale=1.0)
                inc(sc.drain(), s_ac)

            def astat(g):
                par = g % 2
                o = par * GW
                sc.wait_ge(s_ve, VE_I[("vstat", g)])
                sc.activation(st_rstd[0:1, o:o + GW], st_var[0:1, o:o + GW],
                              AF.Ln, bias=float(cfg["EPS"]), scale=1.0)
                sc.drain()
                sc.activation(st_rstd[0:1, o:o + GW], st_rstd[0:1, o:o + GW],
                              AF.Exp, bias=0.0, scale=-0.5)
                inc(sc.drain(), s_ac)

            def bce(g):
                par = g % 2
                o = par * 2 * GW
                sc.wait_ge(s_pe, PE_I[("bc", g)])
                if g >= 2:
                    sc.wait_ge(s_ve, VE_I[("apply", g - 2)])
                pa, pb = (ps[0], ps[1]) if par == 0 else (ps[2], ps[3])
                sc.copy(wk1[:, o:o + GW], pa[:, 0:GW])
                sc.copy(wk1[:, o + GW:o + 2 * GW], pb[:, 0:GW])
                inc(sc.drain(), s_ac)

            fns = dict(sq=sq, astat=astat, bce=bce)
            for kind, g in ac_ord:
                fns[kind](g)
        block.scalar(p1_ac)

        def p1_pe(t):
            def stat(g):
                par = g % 2
                t.wait_ge(s_ac, AC_I[("sq", g)])
                if g >= 2:
                    t.wait_ge(s_ve, VE_I[("vstat", g - 2)])
                pss = ps_sumA[0:1, 0:512] if par == 0 else ps[4][0:1, 0:512]
                for dc in range(NDC):
                    t.matmul(pss[0:1, 0:GW], onebc[:, :],
                             big[:, par * BGR + dc * GW:par * BGR + (dc + 1) * GW],
                             start=(dc == 0), stop=(dc == NDC - 1))
                for dc in range(NDC):
                    mm = t.matmul(pss[0:1, GW:2 * GW], onebc[:, :],
                                  big[:, par * BGR + (NDC + dc) * GW:
                                      par * BGR + (NDC + dc + 1) * GW],
                                  start=(dc == 0), stop=(dc == NDC - 1))
                inc(mm, s_pe)

            def bc(g):
                par = g % 2
                o = par * GW
                t.wait_ge(s_ve, VE_I[("vstat2", g)])
                if g >= 2:
                    t.wait_ge(s_ac, AC_I[("bce", g - 2)])
                pa, pb = (ps[0], ps[1]) if par == 0 else (ps[2], ps[3])
                t.matmul(pa[:, 0:GW], onefr[:, :], st_rstd[0:1, o:o + GW],
                         start=True, stop=True)
                mm = t.matmul(pb[:, 0:GW], onefr[:, :], st_m[0:1, o:o + GW],
                              start=True, stop=True)
                inc(mm, s_pe)

            fns = dict(stat=stat, bc=bc)
            for kind, g in pe_ord:
                fns[kind](g)
        block.tensor(p1_pe)

        for g in range(NP1):
            M[f"p1_apply_{g}"] = (s_ve, VE_I[("apply", g)])
            M[f"p1_st_{g}"] = (s_st, ST_I[("st", g)])
            M[f"p1_stat_{g}"] = (s_pe, PE_I[("stat", g)])

        mark("p1_done_hw", s_hw)
        mark("p1_done_ve", s_ve)

        PH = cfg.get("PHASES", 9)
        if PH < 2:
            meta = dict(cfg=cfg, c=c)
            _finish(block, None)
            return nc, meta
        GP = c.P2GW
        NP2 = c.NP2
        NCC = c.NCC
        NCC2 = max(1, NCC // 2)
        NHALF = NCC // NCC2
        VB = HL * DV
        NHV = 2 if NDC * VB > 32768 else 1
        VB2 = VB // NHV
        NVB = (VB2 + 511) // 512
        NTG = GP // 128

        p1_pe_end = cnt[id(s_pe)]
        p1_st_end = cnt[id(s_st)]
        p2 = dict(sw0=cnt[id(s_sw)], hw0=cnt[id(s_hw)], pe0=cnt[id(s_pe)],
                  ac0=cnt[id(s_ac)], st0=cnt[id(s_st)])
        NU = 2 * NHALF

        def p2_gp(g):
            for u in range(NU):
                pi, hf = divmod(u, NHALF)
                wsrc = wq_l if pi == 0 else wk_l
                if u == 0:
                    g.wait_ge(s_pe, p1_pe_end)
                else:
                    g.wait_ge(s_pe, p2["pe0"] + u * NP2 * NCC2)
                for cc in range(NCC2):
                    ccg = hf * NCC2 + cc
                    inc(g.dma_start(
                        out=big[:, cc * NDC * 128:(cc + 1) * NDC * 128]
                            .rearrange("p (dc i) -> p dc i", i=128),
                        in_=wsrc[0:D, ccg * 128:(ccg + 1) * 128]
                            .rearrange("(dc p) i -> p dc i", p=128)), s_sw, 16)
            nqk = NU * NP2 * NCC2
            for hv in range(NHV):
                g.wait_ge(s_pe, p2["pe0"] + nqk + hv * NP2 * NTG)
                inc(g.dma_start(
                    out=big[:, 0:NDC * VB2].rearrange("p (dc i) -> p dc i", i=VB2),
                    in_=wv_l[0:D, hv * VB2:(hv + 1) * VB2]
                        .rearrange("(dc p) i -> p dc i", p=128)), s_sw, 16)
        block.gpsimd(p2_gp)

        def p2_sync(sync):
            st_i = p2["st0"]
            ac_i = p2["ac0"]
            for u in range(NU):
                pi, hf = divmod(u, NHALF)
                dst = qT_h if pi == 0 else kT_h
                for gI in range(NP2):
                    if u == 0 and gI == 0:
                        sync.wait_ge(s_st, p1_st_end)
                    else:
                        sync.wait_ge(s_pe, p2["pe0"] + u * NP2 * NCC2 + gI * NCC2)
                    inc(sync.dma_start(
                        out=ar2[:, 0:NDC * GP].rearrange("p (dc t) -> p dc t", t=GP),
                        in_=lnT_h[0:D, gI * GP:(gI + 1) * GP]
                            .rearrange("(dc p) t -> p dc t", p=128)), s_hw, 16)
                    for cc in range(NCC2):
                        ccg = hf * NCC2 + cc
                        ac_i += 1
                        sync.wait_ge(s_ac, ac_i)
                        slot = (u * NP2 * NCC2 + gI * NCC2 + cc) % 2
                        inc(sync.dma_start(
                            out=dst[ccg * 128:(ccg + 1) * 128, gI * GP:(gI + 1) * GP],
                            in_=evb[:, slot * GP:(slot + 1) * GP]), s_st, 16)
                        st_i += 16
            nqk = NU * NP2 * NCC2
            for hv in range(NHV):
                for gI in range(NP2):
                    vu = hv * NP2 + gI
                    sync.wait_ge(s_pe, p2["pe0"] + nqk + vu * NTG)
                    inc(sync.dma_start(
                        out=ar2[:, 0:NDC * GP].rearrange("p (dc t) -> p dc t", t=GP),
                        in_=lnT_h[0:D, gI * GP:(gI + 1) * GP]
                            .rearrange("(dc p) t -> p dc t", p=128)), s_hw, 16)
                    for tc in range(NTG):
                        ac_i += 1
                        sync.wait_ge(s_ac, ac_i)
                        inc(sync.dma_start(
                            out=v_h[(gI * NTG + tc) * 128:(gI * NTG + tc + 1) * 128,
                                    hv * VB2:(hv + 1) * VB2],
                            in_=evb[:, 0:VB2]), s_st, 16)
        block.sync(p2_sync)

        def p2_pe(t):
            pe_i = p2["pe0"]
            for u in range(NU):
                t.wait_ge(s_sw, p2["sw0"] + (u + 1) * NCC2 * 16)
                for gI in range(NP2):
                    t.wait_ge(s_hw, p2["hw0"] + (u * NP2 + gI + 1) * 16)
                    for cc in range(NCC2):
                        slot_idx = u * NP2 * NCC2 + gI * NCC2 + cc
                        if slot_idx >= 2:
                            t.wait_ge(s_ac, p2["ac0"] + slot_idx - 1)
                        for dc in range(NDC):
                            mm = t.matmul(
                                ps[slot_idx % 2][:, 0:GP],
                                big[:, (cc * NDC + dc) * 128:(cc * NDC + dc + 1) * 128],
                                ar2[:, dc * GP:(dc + 1) * GP],
                                start=(dc == 0), stop=(dc == NDC - 1))
                        pe_i += 1
                        inc(mm, s_pe)
            nqk = NU * NP2 * NCC2
            for hv in range(NHV):
                t.wait_ge(s_sw, p2["sw0"] + (NU * NCC2 + hv + 1) * 16)
                for gI in range(NP2):
                    t.wait_ge(s_hw, p2["hw0"] + ((NU + hv) * NP2 + gI + 1) * 16)
                    for tc in range(NTG):
                        vi = (hv * NP2 + gI) * NTG + tc
                        if vi >= 1:
                            t.wait_ge(s_ac, p2["ac0"] + nqk + vi)
                        for dc in range(NDC):
                            for nb in range(NVB):
                                w0 = nb * 512
                                w1 = min(VB2, w0 + 512)
                                mm = t.matmul(
                                    ps[nb][:, 0:w1 - w0],
                                    ar2[:, dc * GP + tc * 128:dc * GP + (tc + 1) * 128],
                                    big[:, dc * VB2 + w0:dc * VB2 + w1],
                                    start=(dc == 0), stop=(dc == NDC - 1))
                        inc(mm, s_pe)
        block.tensor(p2_pe)

        def p2_ac(sc):
            st_need = p2["st0"]
            for u in range(NU):
                for gI in range(NP2):
                    for cc in range(NCC2):
                        slot_idx = u * NP2 * NCC2 + gI * NCC2 + cc
                        sc.wait_ge(s_pe, p2["pe0"] + slot_idx + 1)
                        if slot_idx >= 2:
                            sc.wait_ge(s_st, p2["st0"] + slot_idx * 16)
                        slot = slot_idx % 2
                        sc.activation(evb[:, slot * GP:(slot + 1) * GP],
                                      ps[slot_idx % 2][:, 0:GP],
                                      AF.Copy, bias=0.0, scale=1.0)
                        inc(sc.drain(), s_ac)
            nqk = NU * NP2 * NCC2
            for hv in range(NHV):
                for gI in range(NP2):
                    for tc in range(NTG):
                        vi = (hv * NP2 + gI) * NTG + tc
                        sc.wait_ge(s_pe, p2["pe0"] + nqk + vi + 1)
                        if vi >= 1:
                            sc.wait_ge(s_st, p2["st0"] + (nqk + vi) * 16)
                        for nb in range(NVB):
                            w0 = nb * 512
                            w1 = min(VB2, w0 + 512)
                            sc.activation(evb[:, w0:w1], ps[nb][:, 0:w1 - w0],
                                          AF.Copy, bias=0.0, scale=1.0)
                        inc(sc.drain(), s_ac)
        block.scalar(p2_ac)

        p2_pe_end = cnt[id(s_pe)]
        p2_st_end = cnt[id(s_st)]
        p2_ac_end = cnt[id(s_ac)]
        p2_sw_end = cnt[id(s_sw)]
        p2_hw_end = cnt[id(s_hw)]

        if PH < 3:
            meta = dict(cfg=cfg, c=c)
            _finish(block, None)
            return nc, meta
        NQG = S // 512
        KCs = [4 * (qg + 1) for qg in range(NQG)]
        Ps = [k // 2 for k in KCs]
        per_pe = [p + 3 for p in Ps]
        per_ac = [p + 1 for p in Ps]
        per_ve = [2 + 2] * NQG
        PEQ = sum(per_pe)
        ACQ = sum(per_ac)
        VEQ = sum(per_ve)
        C_SCALE = float(cfg["SCALE"])
        C_MSHIFT = -float(cfg["SHIFT"])

        p3 = dict(hw0=cnt[id(s_hw)], pe0=cnt[id(s_pe)], ac0=cnt[id(s_ac)],
                  ve0=cnt[id(s_ve)], sw0=cnt[id(s_sw)])

        pe_pref, ac_pref, ve_pref = [0], [0], [0]
        for qg in range(NQG):
            pe_pref.append(pe_pref[-1] + per_pe[qg])
            ac_pref.append(ac_pref[-1] + per_ac[qg])
            ve_pref.append(ve_pref[-1] + per_ve[qg])

        def p3_pe_pair(h, qg, pi):
            return p3["pe0"] + h * PEQ + pe_pref[qg] + pi + 1

        def p3_pe_sum(h, qg):
            return p3["pe0"] + h * PEQ + pe_pref[qg] + Ps[qg] + 1

        def p3_pe_av(h, qg):
            return p3["pe0"] + h * PEQ + pe_pref[qg] + Ps[qg] + 2

        def p3_pe_bc(h, qg):
            return p3["pe0"] + h * PEQ + pe_pref[qg] + Ps[qg] + 3

        def p3_ac_exp(h, qg, pi):
            return p3["ac0"] + h * ACQ + ac_pref[qg] + pi + 1

        def p3_ac_bce(h, qg):
            return p3["ac0"] + h * ACQ + ac_pref[qg] + Ps[qg] + 1

        def p3_ve_mask(h, qg, pi):
            return p3["ve0"] + h * VEQ + ve_pref[qg] + (pi - (Ps[qg] - 2)) + 1

        def p3_ve_recip(h, qg):
            return p3["ve0"] + h * VEQ + ve_pref[qg] + 3

        def p3_ve_mul(h, qg):
            return p3["ve0"] + h * VEQ + ve_pref[qg] + 4

        PTOT = sum(Ps)

        def pair_global(h, qg, pi):
            return h * PTOT + sum(Ps[:qg]) + pi

        def pair_from_global(idx):
            h, rem = divmod(idx, PTOT)
            qg = 0
            while rem >= Ps[qg]:
                rem -= Ps[qg]
                qg += 1
            return h, qg, rem

        def prev_gi(h, qg, back=1):
            gi = h * NQG + qg - back
            return divmod(gi, NQG) if gi >= 0 else None

        def p3_q(h):
            return wkb[:, 0:S] if h % 2 == 0 else ar2[:, 0:S]

        def p3_k(h):
            return wkb2[:, 0:S] if h % 2 == 0 else ar2[:, S:2 * S]

        def p3_v(h):
            return vbuf[:, 0:S] if h % 2 == 0 else ar2[:, 2 * S:3 * S]

        def p3_sync(sync):
            def ld(h):
                if h == 0:
                    sync.wait_ge(s_st, p2_st_end)
                elif h >= 2:
                    sync.wait_ge(s_pe, p3["pe0"] + (h - 1) * PEQ)
                inc(sync.dma_start(out=p3_q(h),
                                   in_=qT_h[h * 128:(h + 1) * 128, 0:S]), s_hw, 16)
                inc(sync.dma_start(out=p3_k(h),
                                   in_=kT_h[h * 128:(h + 1) * 128, 0:S]), s_hw, 16)
                inc(sync.dma_start(
                    out=p3_v(h).rearrange("p (kc i) -> p kc i", i=DV),
                    in_=v_h[0:S, h * DV:(h + 1) * DV]
                        .rearrange("(kc p) i -> p kc i", p=128)), s_hw, 16)
            ld(0)
            for h in range(1, HL):
                ld(h)
        block.sync(p3_sync)

        def p3_pe(t):
            for h in range(HL):
                t.wait_ge(s_hw, p3["hw0"] + (h + 1) * 48)
                if h == 0:
                    t.wait_ge(s_ac, p2_ac_end)
                for qg in range(NQG):
                    P = Ps[qg]
                    gi = h * NQG + qg
                    avp = ps[4] if gi % 2 == 0 else ps_av1[:, 0:512]

                    def scores(pi):
                        gidx = pair_global(h, qg, pi)
                        if gidx >= 2:
                            ph, pq, pp = pair_from_global(gidx - 2)
                            t.wait_ge(s_ac, p3_ac_exp(ph, pq, pp))
                        sc = ps_sc0 if gidx % 2 == 0 else ps_sc1
                        for half in range(2):
                            kc = 2 * pi + half
                            mm = t.matmul(sc.ap()[:, half * 512:(half + 1) * 512],
                                          p3_k(h)[:, kc * 128:(kc + 1) * 128],
                                          p3_q(h)[:, qg * 512:(qg + 1) * 512],
                                          start=True, stop=True)
                        inc(mm, s_pe)

                    scores(0)
                    if P > 1:
                        scores(1)
                    for pi in range(P):
                        t.wait_ge(s_ac, p3_ac_exp(h, qg, pi))
                        if pi >= P - 2:
                            t.wait_ge(s_ve, p3_ve_mask(h, qg, pi))
                        if pi == 0:
                            pv1 = prev_gi(h, qg, 1)
                            if pv1 is not None:
                                t.wait_ge(s_ve, p3_ve_recip(*pv1))
                            pv2 = prev_gi(h, qg, 2)
                            if pv2 is not None:
                                t.wait_ge(s_ve, p3_ve_mul(*pv2))
                        gi2 = pair_global(h, qg, pi) % 2
                        for half in range(2):
                            kc = 2 * pi + half
                            pslice = pt2[:, gi2 * 1024 + half * 512:
                                         gi2 * 1024 + (half + 1) * 512]
                            last = (kc == KCs[qg] - 1)
                            mm = t.matmul(ps_sumA[0:1, 0:512], onebc[:, :], pslice,
                                          start=(kc == 0), stop=last)
                            if last:
                                inc(mm, s_pe)
                            mm = t.matmul(avp, p3_v(h)[:, kc * 128:(kc + 1) * 128],
                                          pslice, start=(kc == 0), stop=last)
                            if last:
                                inc(mm, s_pe)
                        if pi + 2 < P:
                            scores(pi + 2)
                    t.wait_ge(s_ve, p3_ve_recip(h, qg))
                    pv1 = prev_gi(h, qg, 1)
                    if pv1 is not None:
                        t.wait_ge(s_ac, p3_ac_bce(*pv1))
                    mm = t.matmul(ps_bc[:, 0:512], onefr[:, :],
                                  st_m[0:1, 0:512], start=True, stop=True)
                    inc(mm, s_pe)
        block.tensor(p3_pe)

        def p3_ac(sc):
            for h in range(HL):
                for qg in range(NQG):
                    P = Ps[qg]
                    for pi in range(P):
                        sc.wait_ge(s_pe, p3_pe_pair(h, qg, pi))
                        gidx = pair_global(h, qg, pi)
                        scp = ps_sc0 if gidx % 2 == 0 else ps_sc1
                        sc.activation(pt2[:, (gidx % 2) * 1024:(gidx % 2 + 1) * 1024],
                                      scp.ap()[:, :], AF.Exp,
                                      bias=C_MSHIFT, scale=C_SCALE)
                        inc(sc.drain(), s_ac)
                    gi = h * NQG + qg
                    sc.wait_ge(s_pe, p3_pe_bc(h, qg))
                    pv1 = prev_gi(h, qg, 1)
                    if pv1 is not None:
                        sc.wait_ge(s_ve, p3_ve_mul(*pv1))
                    sc.copy(bcv[:, (gi % 2) * 512:(gi % 2 + 1) * 512],
                            ps_bc[:, 0:512])
                    inc(sc.drain(), s_ac)
        block.scalar(p3_ac)

        def p3_ve(v):
            for h in range(HL):
                for qg in range(NQG):
                    P = Ps[qg]
                    gi = h * NQG + qg
                    avp = ps[4] if gi % 2 == 0 else ps_av1[:, 0:512]
                    for pi in range(P - 2, P):
                        v.wait_ge(s_ac, p3_ac_exp(h, qg, pi))
                        gi2 = pair_global(h, qg, pi) % 2
                        j0 = 2 * (pi - (P - 2))
                        v.tensor_tensor(pt2[:, gi2 * 1024:(gi2 + 1) * 1024],
                                        pt2[:, gi2 * 1024:(gi2 + 1) * 1024],
                                        cmv[:, j0 * 512:(j0 + 2) * 512], ALU.mult)
                        inc(v.drain(), s_ve)
                    v.wait_ge(s_pe, p3_pe_sum(h, qg))
                    v.reciprocal(st_m[0:1, 0:512], ps_sumA[0:1, 0:512])
                    inc(v.drain(), s_ve)
                    v.wait_ge(s_pe, p3_pe_av(h, qg))
                    v.wait_ge(s_ac, p3_ac_bce(h, qg))
                    if gi >= 1:
                        v.wait_ge(s_sw, p3["sw0"] + gi * 16)
                    v.tensor_tensor(attc[:, (qg % 2) * 512:(qg % 2 + 1) * 512],
                                    avp, bcv[:, (gi % 2) * 512:(gi % 2 + 1) * 512],
                                    ALU.mult)
                    inc(v.drain(), s_ve)
        block.vector(p3_ve)

        HLR = HL * 128

        def p3_gp(g):
            g.wait_ge(s_sw, p2_sw_end)
            g.wait_ge(s_hw, M["p0_const"][1])
            with g.register("pregh") as preg:
                g.reg_load(preg, prmv[0:1, 8:9])
                for h in range(HL):
                    for qg in range(NQG):
                        g.wait_ge(s_ve, p3_ve_mul(h, qg))
                        inc(g.dma_start(
                            out=attnT_loc[h * 128:(h + 1) * 128,
                                          qg * 512:(qg + 1) * 512],
                            in_=attc[:, (qg % 2) * 512:(qg % 2 + 1) * 512]), s_sw, 16)
                    g.wait_ge(s_sw, cnt[id(s_sw)])
                    with g.If_eq(preg, 0):
                        g.dma_start(
                            out=sh_attnT[h * 128:(h + 1) * 128, 0:S],
                            in_=attnT_loc[h * 128:(h + 1) * 128, 0:S]).then_inc(s_sw, 16)
                    with g.Else():
                        g.dma_start(
                            out=sh_attnT[HLR + h * 128:HLR + (h + 1) * 128, 0:S],
                            in_=attnT_loc[h * 128:(h + 1) * 128, 0:S]).then_inc(s_sw, 16)
                    cnt[id(s_sw)] += 16
        block.gpsimd(p3_gp)

        p3_pe_end = cnt[id(s_pe)]
        p3_sw_end = cnt[id(s_sw)]
        p3_hw_end = cnt[id(s_hw)]
        p3_ve_end = cnt[id(s_ve)]
        p3_ac_end = cnt[id(s_ac)]

        if PH < 3.5:
            meta = dict(cfg=cfg, c=c)
            _finish(block, None)
            return nc, meta
        def p35_gp(g):
            g.wait_ge(s_sw, p3_sw_end)
            with g.register("preg") as preg, g.register("qreg") as qreg:
                g.reg_load(preg, prmv[0:1, 8:9])
                g.reg_load(qreg, prmv[0:1, 1:2])
                with g.If_eq(preg, 0):
                    g.dma_start(out=sh_flags[0:1, 0:2],
                                in_=prmv[0:1, 4:6]).then_inc(s_sw, 16)
                with g.Else():
                    g.dma_start(out=sh_flags[1:2, 0:2],
                                in_=prmv[0:1, 4:6]).then_inc(s_sw, 16)
                cnt[id(s_sw)] += 16
                g.wait_ge(s_sw, cnt[id(s_sw)])
                with g.register("cond") as cond, g.register("tgt") as tgt, \
                     g.register("fa") as fa, g.register("fb") as fb, \
                     g.register("ea") as ea, g.register("eb") as eb:
                    g.reg_load(ea, prmv[0:1, 4:5])
                    g.reg_load(eb, prmv[0:1, 5:6])
                    g.reg_mov(cond, 1)
                    g.reg_mov(tgt, 0)
                    with g.While(cond):
                        g.reg_add(tgt, tgt, 16)
                        with g.If_eq(preg, 0):
                            g.dma_start(out=flgrd[0:1, 0:2],
                                        in_=sh_flags[1:2, 0:2]).then_inc(s_po1, 16)
                        with g.Else():
                            g.dma_start(out=flgrd[0:1, 0:2],
                                        in_=sh_flags[0:1, 0:2]).then_inc(s_po1, 16)
                        g.wait_ge(s_po1, tgt)
                        g.reg_load(fa, flgrd[0:1, 0:1])
                        g.reg_load(fb, flgrd[0:1, 1:2])
                        g.reg_alu(fa, fa, ea, ALU.is_equal)
                        g.reg_alu(fb, fb, eb, ALU.is_equal)
                        g.reg_alu(cond, fa, fb, ALU.bitwise_and)
                        g.reg_alu(cond, cond, 1, ALU.bitwise_xor)
                def pull(which, k):
                    d0 = which * QS
                    g.dma_start(out=attnT_l2[:, d0:d0 + QS],
                                in_=sh_attnT[0:c.HDV, k * QS:(k + 1) * QS]
                                ).then_inc(s_sw, 16)

                for which in range(2):
                    with g.If_eq(qreg, 0):
                        pull(which, (0 + 2 * which) % 4)
                    with g.Else():
                        with g.If_eq(qreg, 1):
                            pull(which, (1 + 2 * which) % 4)
                        with g.Else():
                            with g.If_eq(qreg, 2):
                                pull(which, (2 + 2 * which) % 4)
                            with g.Else():
                                pull(which, (3 + 2 * which) % 4)
                    cnt[id(s_sw)] += 16
                    g.wait_ge(s_sw, cnt[id(s_sw)])
                    g.sem_inc(s_gp, 1)
                    cnt[id(s_gp)] += 1
        block.gpsimd(p35_gp)
        mark("pair1_ready", s_gp)
        M["pull1_done"] = (s_gp, cnt[id(s_gp)] - 1)

        OQN4 = 4
        OQ4 = O // OQN4
        NHC = c.HDV // 128
        NMT = c.NMT
        NB4 = OQ4 // 512 if OQ4 >= 512 else 1
        W4 = min(OQ4, 512)

        p4 = dict(hw0=cnt[id(s_hw)], sw0=cnt[id(s_sw)], pe0=cnt[id(s_pe)],
                  ve0=cnt[id(s_ve)], st0=cnt[id(s_st)])

        def p4_gp(g):
            for oq in range(OQN4):
                if oq >= 1:
                    g.wait_ge(s_pe, p4["pe0"] + oq * NMT)
                inc(g.dma_start(
                    out=big[:, 0:NHC * OQ4].rearrange("p (hc o) -> p hc o", o=OQ4),
                    in_=wo_full[0:c.HDV, oq * OQ4:(oq + 1) * OQ4]
                        .rearrange("(hc p) o -> p hc o", p=128)), s_sw, 16)
        block.gpsimd(p4_gp)

        def p4_sync(sync):
            st_i = p4["st0"]
            for oq in range(OQN4):
                for tc in range(NMT):
                    ui = oq * NMT + tc
                    if oq == 0:
                        wait(sync, "pull1_done" if tc < NMT // 2 else "pair1_ready")
                    if ui >= 2:
                        sync.wait_ge(s_pe, p4["pe0"] + ui - 1)
                    inc(sync.dma_start(
                        out=attsb[:, (ui % 2) * NHC * 128:(ui % 2 + 1) * NHC * 128]
                            .rearrange("p (hc i) -> p hc i", i=128),
                        in_=attnT_l2[0:c.HDV, tc * 128:(tc + 1) * 128]
                            .rearrange("(hc p) i -> p hc i", p=128)), s_hw, 16)
                    if ui >= 2:
                        sync.wait_ge(s_ve, p4["ve0"] + ui - 1)
                    inc(sync.dma_start(
                        out=xres[:, (ui % 2) * OQ4:(ui % 2 + 1) * OQ4],
                        in_=x_strips[tc * 128:(tc + 1) * 128,
                                     oq * OQ4:(oq + 1) * OQ4]), s_hw, 16)
                    if ui >= 1:
                        po, ptc = divmod(ui - 1, NMT)
                        sync.wait_ge(s_ve, p4["ve0"] + ui)
                        inc(sync.dma_start(
                            out=xaft_h[ptc * 128:(ptc + 1) * 128,
                                       po * OQ4:(po + 1) * OQ4],
                            in_=evf[:, ((ui - 1) % 2) * OQ4:((ui - 1) % 2 + 1) * OQ4]),
                            s_st, 16)
            ui = OQN4 * NMT
            po, ptc = divmod(ui - 1, NMT)
            sync.wait_ge(s_ve, p4["ve0"] + ui)
            inc(sync.dma_start(
                out=xaft_h[ptc * 128:(ptc + 1) * 128, po * OQ4:(po + 1) * OQ4],
                in_=evf[:, ((ui - 1) % 2) * OQ4:((ui - 1) % 2 + 1) * OQ4]), s_st, 16)
        block.sync(p4_sync)

        def p4_pe(t):
            t.wait_ge(s_ac, p3_ac_end)
            t.wait_ge(s_ve, p3_ve_end)
            for oq in range(OQN4):
                t.wait_ge(s_sw, p4["sw0"] + (oq + 1) * 16)
                for tc in range(NMT):
                    ui = oq * NMT + tc
                    t.wait_ge(s_hw, p4["hw0"] + (2 * ui + 1) * 16)
                    if ui >= 2:
                        t.wait_ge(s_ve, p4["ve0"] + ui - 1)
                    pp = ui % 2
                    for hc in range(NHC):
                        for nb in range(NB4):
                            w0 = nb * 512
                            w1 = min(OQ4, w0 + 512)
                            mm = t.matmul(
                                ps[2 * pp + nb][:, 0:w1 - w0],
                                attsb[:, (ui % 2) * NHC * 128 + hc * 128:
                                      (ui % 2) * NHC * 128 + (hc + 1) * 128],
                                big[:, hc * OQ4 + w0:hc * OQ4 + w1],
                                start=(hc == 0), stop=(hc == NHC - 1))
                    inc(mm, s_pe)
        block.tensor(p4_pe)

        def p4_ve(v):
            for oq in range(OQN4):
                for tc in range(NMT):
                    ui = oq * NMT + tc
                    v.wait_ge(s_pe, p4["pe0"] + ui + 1)
                    v.wait_ge(s_hw, p4["hw0"] + (2 * ui + 2) * 16)
                    if ui >= 2:
                        v.wait_ge(s_st, p4["st0"] + ui * 16)
                    pp = ui % 2
                    for nb in range(NB4):
                        w0 = nb * 512
                        w1 = min(OQ4, w0 + 512)
                        v.tensor_tensor(evf[:, pp * OQ4 + w0:pp * OQ4 + w1],
                                        ps[2 * pp + nb][:, 0:w1 - w0],
                                        xres[:, pp * OQ4 + w0:pp * OQ4 + w1],
                                        ALU.add)
                    inc(v.drain(), s_ve)
        block.vector(p4_ve)

        p4_pe_end = cnt[id(s_pe)]
        p4_st_end = cnt[id(s_st)]
        p4_ve_end = cnt[id(s_ve)]
        p4_hw_end = cnt[id(s_hw)]
        p4_sw_end = cnt[id(s_sw)]

        if PH < 4.5:
            meta = dict(cfg=cfg, c=c)
            _finish(block, None)
            return nc, meta
        ACPT = 3 + NDC
        p45 = dict(hw0=cnt[id(s_hw)], pe0=cnt[id(s_pe)], ve0=cnt[id(s_ve)],
                   ac0=cnt[id(s_ac)], st0=cnt[id(s_st)])

        def p45_sync(sync):
            sync.wait_ge(s_st, p4_st_end)
            for tc in range(NMT):
                if tc >= 1:
                    sync.wait_ge(s_ac, p45["ac0"] + ACPT * (tc - 1) + 3)
                    sync.wait_ge(s_st, p45["st0"] + tc * 16)
                inc(sync.dma_start(out=arf[:, 0:D],
                                   in_=xaft_h[tc * 128:(tc + 1) * 128, 0:D]),
                    s_hw, 16)
                sync.wait_ge(s_ac, p45["ac0"] + ACPT * (tc + 1))
                inc(sync.dma_start(
                    out=ln2T_h[0:D, tc * 128:(tc + 1) * 128]
                        .rearrange("(dc p) t -> p dc t", p=128),
                    in_=wkb2[:, 0:NDC * 128].rearrange("p (dc t) -> p dc t", t=128)),
                    s_st, 16)
        block.sync(p45_sync)

        def p45_ve(v):
            for tc in range(NMT):
                v.wait_ge(s_hw, p45["hw0"] + (tc + 1) * 16)
                v.reduce_sum(ln2s[:, 0:1], arf[:, 0:D], AX.X)
                inc(v.drain(), s_ve)
                v.wait_ge(s_ac, p45["ac0"] + ACPT * tc + 1)
                v.reduce_sum(ln2s[:, 1:2], big[:, 0:D], AX.X)
                inc(v.drain(), s_ve)
                v.tensor_scalar_mul(ln2s[:, 2:3], ln2s[:, 0:1], 1.0 / D)
                v.tensor_scalar_mul(ln2s[:, 3:4], ln2s[:, 1:2], 1.0 / D)
                v.drain()
                v.tensor_tensor(ln2s[:, 4:5], ln2s[:, 2:3], ln2s[:, 2:3], ALU.mult)
                v.drain()
                v.tensor_tensor(ln2s[:, 4:5], ln2s[:, 3:4], ln2s[:, 4:5], ALU.subtract)
                inc(v.drain(), s_ve)
                v.wait_ge(s_ac, p45["ac0"] + ACPT * tc + 2)
                v.tensor_tensor(ln2s[:, 6:7], ln2s[:, 2:3], ln2s[:, 5:6], ALU.mult)
                v.drain()
                v.tensor_scalar_mul(ln2s[:, 6:7], ln2s[:, 6:7], -1.0)
                inc(v.drain(), s_ve)
        block.vector(p45_ve)

        def p45_ac(sc):
            sc.wait_ge(s_ve, p4_ve_end)
            for tc in range(NMT):
                v0 = p45["ve0"] + 4 * tc
                sc.wait_ge(s_ve, v0 + 1)
                for dc in range(NDC):
                    sc.activation(big[:, dc * 128:(dc + 1) * 128],
                                  arf[:, dc * 128:(dc + 1) * 128],
                                  AF.Square, bias=0.0, scale=1.0)
                inc(sc.drain(), s_ac)
                sc.wait_ge(s_ve, v0 + 3)
                sc.activation(ln2s[:, 5:6], ln2s[:, 4:5], AF.Ln,
                              bias=float(cfg["EPS"]), scale=1.0)
                sc.drain()
                sc.activation(ln2s[:, 5:6], ln2s[:, 5:6], AF.Exp,
                              bias=0.0, scale=-0.5)
                inc(sc.drain(), s_ac)
                sc.wait_ge(s_ve, v0 + 4)
                if tc >= 1:
                    sc.wait_ge(s_pe, p45["pe0"] + tc * NDC)
                sc.activation(wkb[:, 0:D], arf[:, 0:D], AF.Identity,
                              bias=ln2s[:, 6:7], scale=ln2s[:, 5:6])
                inc(sc.drain(), s_ac)
                for dc in range(NDC):
                    sc.wait_ge(s_pe, p45["pe0"] + tc * NDC + dc + 1)
                    pt = ps_tr if dc % 2 == 0 else ps_tr2
                    sc.copy(wkb2[:, dc * 128:(dc + 1) * 128], pt[:, 0:128])
                    inc(sc.drain(), s_ac)
        block.scalar(p45_ac)

        def p45_pe(t):
            t.wait_ge(s_ve, p4_ve_end)
            for tc in range(NMT):
                t.wait_ge(s_ac, p45["ac0"] + ACPT * tc + 3)
                for dc in range(NDC):
                    if tc * NDC + dc >= 2:
                        pidx = tc * NDC + dc - 2
                        ptc, pdc = divmod(pidx, NDC)
                        t.wait_ge(s_ac, p45["ac0"] + ACPT * ptc + 3 + pdc + 1)
                    pt = ps_tr if dc % 2 == 0 else ps_tr2
                    tr = t.transpose(pt[:, 0:128], wkb[:, dc * 128:(dc + 1) * 128],
                                     idn[:, :])
                    inc(tr, s_pe)
        block.tensor(p45_pe)

        p45_pe_end = cnt[id(s_pe)]
        p45_st_end = cnt[id(s_st)]
        p45_ac_end = cnt[id(s_ac)]
        p45_ve_end = cnt[id(s_ve)]
        p45_hw_end = cnt[id(s_hw)]

        if PH < 5:
            meta = dict(cfg=cfg, c=c)
            _finish(block, None)
            return nc, meta
        p5 = dict(hw0=cnt[id(s_hw)], sw0=cnt[id(s_sw)], pe0=cnt[id(s_pe)],
                  ve0=cnt[id(s_ve)], ac0=cnt[id(s_ac)], gp0=cnt[id(s_gp)])

        def p5_sync(sync):
            sync.wait_ge(s_st, p45_st_end)
            for dc in range(NDC):
                if dc >= 2:
                    sync.wait_ge(s_ve, p5["ve0"] + dc - 1)
                inc(sync.dma_start(
                    out=big[:, (dc % 2) * c.MYTOK:(dc % 2 + 1) * c.MYTOK],
                    in_=ln2T_h[dc * 128:(dc + 1) * 128, 0:c.MYTOK]), s_hw, 16)
        block.sync(p5_sync)

        def p5_ve(v):
            v.wait_ge(s_sw, p5["sw0"] + 16)
            for dc in range(NDC):
                v.wait_ge(s_hw, p5["hw0"] + (dc + 1) * 16)
                v.tensor_tensor(wkb[:, 0:c.MYTOK],
                                big[:, (dc % 2) * c.MYTOK:(dc % 2 + 1) * c.MYTOK],
                                wmv[:, 0:c.MYTOK], ALU.mult)
                v.drain()
                v.reduce_sum(qpart[:, dc:dc + 1], wkb[:, 0:c.MYTOK], AX.X)
                inc(v.drain(), s_ve)
        block.vector(p5_ve)

        def p5_gp(g):
            g.wait_ge(s_st, p45_st_end)
            inc(g.dma_start(out=wmv[:, 0:c.MYTOK], in_=wmaskt[:, :]), s_sw, 16)
            g.wait_ge(s_ve, p5["ve0"] + NDC)
            with g.register("pr5") as preg:
                g.reg_load(preg, prmv[0:1, 8:9])
                with g.If_eq(preg, 0):
                    g.dma_start(out=sh_qpart[0:128, 0:NDC],
                                in_=qpart[:, 0:NDC]).then_inc(s_sw, 16)
                with g.Else():
                    g.dma_start(out=sh_qpart[128:256, 0:NDC],
                                in_=qpart[:, 0:NDC]).then_inc(s_sw, 16)
                cnt[id(s_sw)] += 16
                g.wait_ge(s_sw, cnt[id(s_sw)])
                with g.If_eq(preg, 0):
                    g.dma_start(out=sh_flags[0:1, 2:4],
                                in_=prmv[0:1, 6:8]).then_inc(s_sw, 16)
                with g.Else():
                    g.dma_start(out=sh_flags[1:2, 2:4],
                                in_=prmv[0:1, 6:8]).then_inc(s_sw, 16)
                cnt[id(s_sw)] += 16
                g.wait_ge(s_sw, cnt[id(s_sw)])
                with g.register("cond2") as cond, g.register("tgt2") as tgt, \
                     g.register("fa2") as fa, g.register("fb2") as fb, \
                     g.register("ea2") as ea, g.register("eb2") as eb:
                    g.reg_load(ea, prmv[0:1, 6:7])
                    g.reg_load(eb, prmv[0:1, 7:8])
                    g.reg_mov(cond, 1)
                    g.reg_mov(tgt, 0)
                    with g.While(cond):
                        g.reg_add(tgt, tgt, 16)
                        with g.If_eq(preg, 0):
                            g.dma_start(out=flgrd[0:1, 2:4],
                                        in_=sh_flags[1:2, 2:4]).then_inc(s_po2, 16)
                        with g.Else():
                            g.dma_start(out=flgrd[0:1, 2:4],
                                        in_=sh_flags[0:1, 2:4]).then_inc(s_po2, 16)
                        g.wait_ge(s_po2, tgt)
                        g.reg_load(fa, flgrd[0:1, 2:3])
                        g.reg_load(fb, flgrd[0:1, 3:4])
                        g.reg_alu(fa, fa, ea, ALU.is_equal)
                        g.reg_alu(fb, fb, eb, ALU.is_equal)
                        g.reg_alu(cond, fa, fb, ALU.bitwise_and)
                        g.reg_alu(cond, cond, 1, ALU.bitwise_xor)
                with g.If_eq(preg, 0):
                    g.dma_start(out=qpeer[:, 0:NDC],
                                in_=sh_qpart[128:256, 0:NDC]).then_inc(s_sw, 16)
                with g.Else():
                    g.dma_start(out=qpeer[:, 0:NDC],
                                in_=sh_qpart[0:128, 0:NDC]).then_inc(s_sw, 16)
                cnt[id(s_sw)] += 16
                g.wait_ge(s_sw, cnt[id(s_sw)])
            g.sem_inc(s_gp, 1)
            cnt[id(s_gp)] += 1
        block.gpsimd(p5_gp)
        mark("pair2_ready", s_gp)

        def p5_ve2(v):
            wait(v, "pair2_ready")
            v.tensor_tensor(qpart[:, 0:NDC], qpart[:, 0:NDC], qpeer[:, 0:NDC],
                            ALU.add)
            v.drain()
            for e in range(E):
                v.tensor_tensor(wk1[:, 0:NDC], qpart[:, 0:NDC],
                                rwv[:, e * NDC:(e + 1) * NDC], ALU.mult)
                v.drain()
                v.reduce_sum(wk1[:, NDC + e:NDC + e + 1], wk1[:, 0:NDC], AX.X)
                v.drain()
            inc(v.engine_nop(), s_ve)
        block.vector(p5_ve2)
        mark("p5_dots", s_ve)

        def p5_pe(t):
            wait(t, "p5_dots")
            t.wait_ge(s_ac, p45_ac_end)
            mm = t.matmul(ps_sumA[0:1, 0:E], onefc[:, 0:1],
                          wk1[:, NDC:NDC + E], start=True, stop=True)
            inc(mm, s_pe)
        block.tensor(p5_pe)
        mark("p5_logits_mm", s_pe)

        def p5_ve3(v):
            wait(v, "p5_logits_mm")
            v.tensor_scalar_mul(r8[0:1, 0:E], ps_sumA[0:1, 0:E], 1.0 / c.WIN_N)
            v.drain()
            v.tensor_tensor(r8[0:1, 0:E], r8[0:1, 0:E], rbv[0:1, 0:E], ALU.add)
            v.drain()
            v.reduce_max(r8[0:1, E:E + 1], r8[0:1, 0:E], AX.X)
            v.drain()
            v.tensor_tensor(r8[0:1, 8:8 + E], r8[0:1, 0:E],
                            r8[0:1, E:E + 1].broadcast_to([1, E]), ALU.subtract)
            inc(v.drain(), s_ve)
        block.vector(p5_ve3)
        mark("p5_lg", s_ve)

        def p5_ac(sc):
            wait(sc, "p5_lg")
            sc.activation(r8[0:1, 16:16 + E], r8[0:1, 8:8 + E], AF.Exp,
                          bias=0.0, scale=1.0)
            inc(sc.drain(), s_ac)
        block.scalar(p5_ac)
        mark("p5_exp", s_ac)

        def p5_ve4(v):
            wait(v, "p5_exp")
            v.reduce_sum(r8[0:1, 24:25], r8[0:1, 16:16 + E], AX.X)
            v.drain()
            v.reciprocal(r8[0:1, 25:26], r8[0:1, 24:25])
            v.drain()
            v.tensor_tensor(r8[0:1, 32:32 + E], r8[0:1, 16:16 + E],
                            r8[0:1, 25:26].broadcast_to([1, E]), ALU.mult)
            v.drain()
            v.tensor_copy(r8b[0:1, 0:E], r8[0:1, 32:32 + E])
            inc(v.drain(), s_ve)
        block.vector(p5_ve4)
        mark("p5_routing", s_ve)

        def p5_pe2(t):
            wait(t, "p5_routing")
            t.wait_ge(s_ac, p45_ac_end)
            tr = t.transpose(ps_tr[0:E, 0:1], r8b[0:1, 0:E], idn[0:1, 0:1])
            inc(tr, s_pe)
        block.tensor(p5_pe2)
        mark("p5_rT", s_pe)

        def p5_ac2(sc):
            wait(sc, "p5_rT")
            sc.copy(r8b[0:E, 1:2], ps_tr[0:E, 0:1])
            inc(sc.drain(), s_ac)
        block.scalar(p5_ac2)
        mark("p5_rTs", s_ac)

        def p5_pe3(t):
            wait(t, "p5_rTs")
            mm = t.matmul(ps[4][:, 0:1], repv[0:E, 0:ER], r8b[0:E, 1:2],
                          start=True, stop=True)
            inc(mm, s_pe)
        block.tensor(p5_pe3)
        mark("p5_rexp_mm", s_pe)

        def p5_ac3(sc):
            wait(sc, "p5_rexp_mm")
            sc.copy(rexp[:, 0:1], ps[4][:, 0:1])
            inc(sc.drain(), s_ac)
        block.scalar(p5_ac3)
        mark("p5_rexp", s_ac)

        if PH < 6:
            meta = dict(cfg=cfg, c=c)
            _finish(block, None)
            return nc, meta
        NFT = c.NFT
        OQ6 = c.OQ
        NB6 = max(1, OQ6 // 512)
        p6 = dict(hw0=cnt[id(s_hw)], sw0=cnt[id(s_sw)], pe0=cnt[id(s_pe)],
                  ve0=cnt[id(s_ve)], ac0=cnt[id(s_ac)], st0=cnt[id(s_st)])

        def p6_sync(sync):
            sync.wait_ge(s_st, p45_st_end)
            sync.wait_ge(s_ve, p5["ve0"] + NDC)
            for dc in range(NDC):
                inc(sync.dma_start(
                    out=big[:, dc * QS:(dc + 1) * QS],
                    in_=ln2T_h[dc * 128:(dc + 1) * 128, 0:QS]), s_hw, 16)
            for oq in range(c.OQN):
                for tc in range(NFT):
                    ui = oq * NFT + tc
                    if ui >= 2:
                        sync.wait_ge(s_ve, p6["ve0"] + ui - 1)
                    inc(sync.dma_start(
                        out=xres[:, (ui % 2) * OQ6:(ui % 2) * OQ6 + OQ6],
                        in_=xaft_h[tc * 128:(tc + 1) * 128,
                                   oq * OQ6:(oq + 1) * OQ6]), s_hw, 16)
                    if ui >= 1:
                        po, ptc = divmod(ui - 1, NFT)
                        sync.wait_ge(s_ve, p6["ve0"] + ui)
                        inc(sync.dma_start(
                            out=out_t[ptc * 128:(ptc + 1) * 128,
                                      po * OQ6:(po + 1) * OQ6],
                            in_=evf[:, ((ui - 1) % 2) * OQ6:((ui - 1) % 2) * OQ6 + OQ6]),
                            s_st, 16)
            ui = c.OQN * NFT
            po, ptc = divmod(ui - 1, NFT)
            sync.wait_ge(s_ve, p6["ve0"] + ui)
            inc(sync.dma_start(
                out=out_t[ptc * 128:(ptc + 1) * 128, po * OQ6:(po + 1) * OQ6],
                in_=evf[:, ((ui - 1) % 2) * OQ6:((ui - 1) % 2) * OQ6 + OQ6]), s_st, 16)
            sync.wait_ge(s_st, cnt[id(s_st)])
        block.sync(p6_sync)

        def p6_gp(g):
            g.wait_ge(s_pe, p4_pe_end)
            inc(g.dma_start(out=atv[:, 0:NDC * ER]
                            .rearrange("p (dc e) -> p dc e", e=ER),
                            in_=ATt.rearrange("(dc p) e -> p dc e", p=128)), s_sw, 16)
            inc(g.dma_start(out=attsb[0:1, 0:O], in_=bTt[:, :]), s_sw, 16)
            inc(g.dma_start(out=attsb[:, O:2 * O], in_=BTt[:, :]), s_sw, 16)
            for oq in range(c.OQN):
                if oq >= 1:
                    g.wait_ge(s_pe, p6["pe0"] + 2 * NFT + oq * NFT)
                inc(g.dma_start(
                    out=ar2[:, 0:NDC * OQ6].rearrange("p (dc o) -> p dc o", o=OQ6),
                    in_=WTt[0:D, oq * OQ6:(oq + 1) * OQ6]
                        .rearrange("(dc p) o -> p dc o", p=128)), s_sw, 16)
        block.gpsimd(p6_gp)

        def p6_pe(t):
            t.wait_ge(s_hw, p6["hw0"] + NDC * 16)
            t.wait_ge(s_sw, p6["sw0"] + 3 * 16)
            wait(t, "p5_rexp")
            for tc in range(NFT):
                if tc >= 1:
                    t.wait_ge(s_ac, p6["ac0"] + 2 * tc - 1)
                for dc in range(NDC):
                    mm = t.matmul(ps[4][:, 0:ER],
                                  big[:, dc * QS + tc * 128:dc * QS + (tc + 1) * 128],
                                  atv[:, dc * ER:(dc + 1) * ER],
                                  start=(dc == 0), stop=(dc == NDC - 1))
                inc(mm, s_pe)
                t.wait_ge(s_ac, p6["ac0"] + 2 * tc + 1)
                tr = t.transpose(ps_tr[:, 0:128], wkb[:, tc * 128:(tc + 1) * 128],
                                 idn[:, :])
                inc(tr, s_pe)
            for oq in range(c.OQN):
                t.wait_ge(s_sw, p6["sw0"] + (3 + oq + 1) * 16)
                for tc in range(NFT):
                    ui = oq * NFT + tc
                    if ui >= 2:
                        t.wait_ge(s_ve, p6["ve0"] + ui - 1)
                    pp = ui % 2
                    t.wait_ge(s_ac, p6["ac0"] + 2 * tc + 2)
                    t.matmul(ps[2 * pp + 0][:, 0:min(512, OQ6)],
                             onebr[0:1, :],
                             attsb[0:1, oq * OQ6:oq * OQ6 + min(512, OQ6)],
                             start=True, stop=False)
                    if NB6 > 1:
                        t.matmul(ps[2 * pp + 1][:, 0:512], onebr[0:1, :],
                                 attsb[0:1, oq * OQ6 + 512:oq * OQ6 + 1024],
                                 start=True, stop=False)
                    for dc in range(NDC):
                        for nb in range(NB6):
                            w0 = nb * 512
                            w1 = min(OQ6, w0 + 512)
                            t.matmul(ps[2 * pp + nb][:, 0:w1 - w0],
                                     big[:, dc * QS + tc * 128:dc * QS + (tc + 1) * 128],
                                     ar2[:, dc * OQ6 + w0:dc * OQ6 + w1],
                                     start=False, stop=False)
                    for nb in range(NB6):
                        w0 = nb * 512
                        w1 = min(OQ6, w0 + 512)
                        mm = t.matmul(ps[2 * pp + nb][:, 0:w1 - w0],
                                      tts[:, tc * 128:(tc + 1) * 128],
                                      attsb[:, O + oq * OQ6 + w0:O + oq * OQ6 + w1],
                                      start=False, stop=True)
                    inc(mm, s_pe)
        block.tensor(p6_pe)

        def p6_ac(sc):
            for tc in range(NFT):
                sc.wait_ge(s_pe, p6["pe0"] + 2 * tc + 1)
                sc.copy(wkb[:, tc * 128:(tc + 1) * 128], ps[4][:, 0:ER])
                inc(sc.drain(), s_ac)
                sc.wait_ge(s_pe, p6["pe0"] + 2 * tc + 2)
                sc.activation(tts[:, tc * 128:(tc + 1) * 128], ps_tr[:, 0:128],
                              AF.Identity, bias=0.0, scale=rexp[:, 0:1])
                inc(sc.drain(), s_ac)
        block.scalar(p6_ac)

        def p6_ve(v):
            for oq in range(c.OQN):
                for tc in range(NFT):
                    ui = oq * NFT + tc
                    v.wait_ge(s_pe, p6["pe0"] + 2 * NFT + ui + 1)
                    v.wait_ge(s_hw, p6["hw0"] + (NDC + ui + 1) * 16)
                    if ui >= 2:
                        v.wait_ge(s_st, p6["st0"] + ui * 16)
                    pp = ui % 2
                    for nb in range(NB6):
                        w0 = nb * 512
                        w1 = min(OQ6, w0 + 512)
                        v.tensor_tensor(evf[:, pp * OQ6 + w0:pp * OQ6 + w1],
                                        ps[2 * pp + nb][:, 0:w1 - w0],
                                        xres[:, pp * OQ6 + w0:pp * OQ6 + w1],
                                        ALU.add)
                    inc(v.drain(), s_ve)
        block.vector(p6_ve)

        meta = dict(cfg=cfg, c=c)
    return nc, meta



def host_slices(cfg, inputs, core):
    c = C(cfg)
    b, p, q = core // 4, core % 2, core % 4
    s_loc = q // 2
    x = np.ascontiguousarray(inputs["L0_attn_hidden_states"][b])
    wq = inputs["L0_attn_wq"]
    wkv = inputs["L0_attn_wkv"]
    wo = inputs["L0_attn_wo"]
    W = inputs["L0_ffn_W"]
    bias = inputs["L0_ffn_b"]
    A = inputs["L0_ffn_A"]
    B = inputs["L0_ffn_B"]
    rW = inputs["L0_ffn_router_W"]
    rb = inputs["L0_ffn_router_b"]
    S, D, HL, DN, DV, E, R, O = c.S, c.D, c.HL, c.DN, c.DV, c.E, c.R, c.O
    NH = 2 * HL
    DQG = wq.shape[1] // NH
    myheads = list(range(p * HL, (p + 1) * HL))
    own = [p * c.QS, 2 * c.QS + p * c.QS]
    ffn0 = q * c.QS
    strips = [ffn0, own[0] if own[1] == ffn0 else own[1]]

    xT = np.ascontiguousarray(x.T)
    x_strips = np.ascontiguousarray(
        np.concatenate([x[s0:s0 + c.QS] for s0 in strips], axis=0))
    wq_l = np.ascontiguousarray(
        np.concatenate([wq[:, h * DQG:h * DQG + DN] for h in myheads], axis=1))
    wk_l = np.ascontiguousarray(
        np.concatenate([wkv[:, h * DN:(h + 1) * DN] for h in myheads], axis=1))
    vbase = NH * DN
    wv_l = np.ascontiguousarray(
        np.concatenate([wkv[:, vbase + h * DV:vbase + (h + 1) * DV] for h in myheads], axis=1))
    wo_full_a = np.ascontiguousarray(wo)
    WT = np.ascontiguousarray(W.T)
    bT = np.ascontiguousarray(bias[None, :])
    AT = np.ascontiguousarray(A.transpose(2, 0, 1).reshape(D, E * R))
    BT = np.ascontiguousarray(B.transpose(0, 2, 1).reshape(E * R, O))
    rwH = np.ascontiguousarray(
        rW.reshape(E, c.NDC, 128).transpose(2, 0, 1).reshape(128, E * c.NDC))
    rbT = np.ascontiguousarray(rb[None, :])
    gtok = np.concatenate([np.arange(s0, s0 + c.QS) for s0 in strips])
    wm = ((gtok >= c.W0) & (gtok < c.W1)).astype(np.float32)[None, :]
    wmask = np.ascontiguousarray(np.tile(wm, (128, 1)))
    salt = np.random.randint(1, 2**31 - 1, size=4, dtype=np.int32)
    prm = np.array([[p,
                     strips[0] // c.QS, strips[1] // c.QS,
                     0,
                     salt[0], salt[1], salt[2], salt[3],
                     p, 1 - p, 0, 0]], dtype=np.int32)
    return dict(xT=xT, x_strips=x_strips, wq_l=wq_l, wk_l=wk_l, wv_l=wv_l,
                wo_full=wo_full_a, WTt=WT, bTt=bT, ATt=AT, BTt=BT,
                rwHt=rwH, rbTt=rbT, wmaskt=wmask, prm=prm)


_SALT_SHARED = None


def kernel(**inputs):
    cfg = full_cfg()
    nc, meta = build_program(cfg)
    in_maps = []
    for corex in range(8):
        in_maps.append(host_slices(cfg, inputs, corex))
    for k in range(4):
        in_maps[2 * k + 1]["prm"][0, 4:8] = in_maps[2 * k]["prm"][0, 4:8]
    res = run_bass_kernel_spmd(nc, in_maps, core_ids=list(range(8)))
    c = C(cfg)
    out = np.zeros((2, cfg["S"], cfg["O"]), dtype=np.float32)
    for corex in range(8):
        b, q = corex // 4, corex % 4
        out[b, q * c.QS:(q + 1) * c.QS, :] = np.asarray(res.results[corex]["out"])
    return out
